# version 15
# baseline (speedup 1.0000x reference)
import numpy as np
from contextlib import ExitStack

import concourse.bass as bass
import concourse.mybir as mybir
from concourse.bass_utils import run_bass_kernel_spmd

F32 = mybir.dt.float32
BF16 = mybir.dt.bfloat16
F16 = mybir.dt.float16
AF = mybir.ActivationFunctionType
ALU = mybir.AluOpType
AX = mybir.AxisListType.X

D = 1024
HID = 4096
KB = D // 128
HB = HID // 128
NHEAD = 16
DH = 64
EPS_LN = 1e-5
ATTN_SCALE = 1.0 / (DH * DH * NHEAD)
MAGIC = 12582912.0

ENGINES = ("tensor", "vector", "scalar", "gpsimd", "sync")


class Prog:

    def __init__(self, nc, ctx):
        self.nc = nc
        self.ctx = ctx
        self.ops = {e: [] for e in ENGINES}
        self.cnt = {e: 0 for e in ENGINES}
        self.esem = {
            e: ctx.enter_context(nc.semaphore(f"ms_{e}")) for e in ENGINES
        }
        self.dslots = {}
        self.waited = {e: {} for e in ENGINES}

    def dma_slot(self, name):
        if name not in self.dslots:
            self.dslots[name] = [
                self.ctx.enter_context(self.nc.semaphore(f"d_{name}")), 0
            ]
        return self.dslots[name]

    def tok(self, eng):
        return ("e", eng, self.cnt[eng])

    def dtok(self, name):
        return ("d", name, self.dslots[name][1])

    def op(self, eng, fn, waits=(), inc=True):
        waits = [w for w in waits if w is not None]
        if inc:
            self.cnt[eng] += 1
        my = self.cnt[eng]
        sem = self.esem[eng]
        wl = []
        hw = self.waited[eng]
        for w in waits:
            if w[0] == "e":
                key = ("e", w[1])
                wsem, val = self.esem[w[1]], w[2]
            else:
                key = ("d", w[1])
                wsem, val = self.dslots[w[1]][0], w[2]
            if val <= 0:
                continue
            if hw.get(key, -1) >= val:
                continue
            hw[key] = val
            wl.append((wsem, val))

        def closure(e, fn=fn, wl=wl, inc=inc, sem=sem):
            for wsem, val in wl:
                e.wait_ge(wsem, val)
            inst = fn(e)
            if inc:
                assert inst is not None, "step fn must return its last instruction"
                inst.then_inc(sem, 1)

        self.ops[eng].append(closure)
        return ("e", eng, my)

    def wait_self(self, eng, token):
        assert token[0] == "e" and token[1] == eng
        sem, val = self.esem[eng], token[2]
        hw = self.waited[eng]
        key = ("e", eng)
        if hw.get(key, -1) >= val or val <= 0:
            return
        hw[key] = val
        self.ops[eng].append(lambda e, s=sem, v=val: e.wait_ge(s, v))

    def dma(self, eng, slot_name, dst, src, waits=(), transpose=False):
        slot = self.dma_slot(slot_name)
        slot[1] += 16
        target = slot[1]
        sem = slot[0]
        waits = [w for w in waits if w is not None]
        wl = []
        hw = self.waited[eng]
        for w in waits:
            if w[0] == "e":
                key = ("e", w[1])
                wsem, val = self.esem[w[1]], w[2]
            else:
                key = ("d", w[1])
                wsem, val = self.dslots[w[1]][0], w[2]
            if val <= 0 or hw.get(key, -1) >= val:
                continue
            if w[0] == "e" and w[1] == eng:
                pass
            hw[key] = val
            wl.append((wsem, val))

        def closure(e, dst=dst, src=src, wl=wl, sem=sem, transpose=transpose):
            for wsem, val in wl:
                e.wait_ge(wsem, val)
            if transpose:
                e.dma_start(dst, src, transpose=True).then_inc(sem, 16)
            else:
                e.dma_start(dst, src).then_inc(sem, 16)

        self.ops[eng].append(closure)
        return ("d", slot_name, target)

    def emit(self):
        block = self.ctx.enter_context(self.nc.Block())
        for ename in ENGINES:
            ops = self.ops[ename]
            if not ops:
                continue
            section = getattr(block, ename)

            def make(ops):
                def body(e):
                    for fn in ops:
                        fn(e)
                return body

            section(make(list(ops)))



def prep_weights(inp):
    w = {}
    k3 = np.asarray(inp["L0_attn_kernel"])
    for nm, idx in (("wq", 0), ("wk", 1), ("wv", 2)):
        w[nm] = np.ascontiguousarray(
            k3[idx].transpose(1, 0, 2).reshape(D, NHEAD * DH)
        )
    wo = np.asarray(inp["L0_attn_WO"])[0]
    w["wo"] = np.ascontiguousarray(
        wo.reshape(DH, NHEAD, D).transpose(1, 0, 2).reshape(D, D)
    )

    def mmaj(wt, kb):
        k, m = wt.shape
        return np.ascontiguousarray(
            wt.reshape(kb, 128, m // 128, 128).transpose(2, 1, 0, 3)
            .reshape(m // 128, 128, kb * 128)
        )

    w["w1m"] = mmaj(np.asarray(inp["L0_ffn_w1_w"]).T, KB)
    w["w2m"] = mmaj(np.asarray(inp["L0_ffn_w2_w"]).T, KB)
    w["h1m"] = mmaj(np.asarray(inp["L0_ffn_h1_w"]).T, HB)
    w["h2m"] = mmaj(np.asarray(inp["L0_ffn_h2_w"]).T, HB)
    w["w3T"] = np.ascontiguousarray(np.asarray(inp["L0_ffn_w3_w"]).T)

    def colmaj_scale(s):
        n = s.shape[0]
        return np.ascontiguousarray(s.reshape(n // 128, 128).T)

    w["w1s"] = colmaj_scale(np.asarray(inp["L0_ffn_w1_s"]))
    w["w2s"] = colmaj_scale(np.asarray(inp["L0_ffn_w2_s"]))
    w["h1s"] = colmaj_scale(np.asarray(inp["L0_ffn_h1_s"]))
    w["h2s"] = colmaj_scale(np.asarray(inp["L0_ffn_h2_s"]))
    w["w3srow"] = np.ascontiguousarray(np.asarray(inp["L0_ffn_w3_s"])[None, :])

    w["winT"] = np.ascontiguousarray(np.asarray(inp["L1_attn_W_in"]))
    w["convw"] = np.ascontiguousarray(
        np.asarray(inp["L1_attn_conv_w"]).reshape(D, 4)
        .reshape(KB, 128, 4).transpose(1, 0, 2).reshape(128, KB * 4)
    )
    w["convb"] = colmaj_scale(np.asarray(inp["L1_attn_conv_b"]))
    wxp = np.asarray(inp["L1_attn_W_xproj"])
    wxp65 = np.zeros((D, 65), np.float32)
    wxp65[:, 0:16] = wxp[:, 0:16]
    wxp65[:, 32:48] = wxp[:, 16:32]
    wxp65[:, 64] = wxp[:, 32]
    w["wxp"] = wxp65
    w["logA"] = np.ascontiguousarray(
        np.asarray(inp["L1_attn_log_A"]).reshape(KB, 128, 16)
        .transpose(1, 0, 2).reshape(128, KB * 16)
    )
    w["dparam"] = colmaj_scale(np.asarray(inp["L1_attn_D_param"]))
    w["wout"] = np.ascontiguousarray(np.asarray(inp["L1_attn_W_out"]))

    w["wfcm"] = mmaj(np.asarray(inp["L1_ffn_W_fc"]).T, KB)
    w["lnw"] = colmaj_scale(np.asarray(inp["L1_ffn_ln_w"]))
    w["lnb"] = colmaj_scale(np.asarray(inp["L1_ffn_ln_b"]))
    w["wprojT"] = np.ascontiguousarray(np.asarray(inp["L1_ffn_W_proj"]).T)
    return w


WEIGHT_SPECS = {
    "wq": ((D, D), F32), "wk": ((D, D), F32), "wv": ((D, D), F32),
    "wo": ((D, D), F32),
    "w1m": ((HB, 128, KB * 128), F32), "w2m": ((HB, 128, KB * 128), F32),
    "h1m": ((HB, 128, HB * 128), F32), "h2m": ((HB, 128, HB * 128), F32),
    "w3T": ((HID, D), F32),
    "w1s": ((128, HB), F32), "w2s": ((128, HB), F32),
    "h1s": ((128, HB), F32), "h2s": ((128, HB), F32),
    "w3srow": ((1, D), F32),
    "winT": ((D, 2 * D), F32),
    "convw": ((128, KB * 4), F32), "convb": ((128, KB), F32),
    "wxp": ((D, 65), F32),
    "logA": ((128, KB * 16), F32),
    "dparam": ((128, KB), F32),
    "wout": ((D, D), F32),
    "wfcm": ((HB, 128, KB * 128), F32),
    "lnw": ((128, HB), F32), "lnb": ((128, HB), F32),
    "wprojT": ((HID, D), F32),
}


def host_consts():
    c = {}
    c["onesrow"] = np.ones((1, 128), np.float32)
    c["onescol"] = np.ones((128, 1), np.float32)
    import ml_dtypes
    c["onescol16"] = np.ones((128, 1), ml_dtypes.bfloat16)
    rep = np.zeros((16, 16 * 128), np.float32)
    for n in range(16):
        rep[n, n * 128:(n + 1) * 128] = 1.0
    c["repmat"] = rep
    pool = np.zeros((128, 8), np.float32)
    pool[:, 0] = EPS_LN
    pool[:, 1] = -0.5
    pool[:, 2] = ATTN_SCALE
    pool[:, 3] = -1.0
    pool[:, 4] = 1.0
    c["cpool"] = pool
    return c


CONST_SPECS = {
    "onesrow": ((1, 128), F32),
    "onescol": ((128, 1), F32),
    "onescol16": ((128, 1), BF16),
    "repmat": ((16, 16 * 128), F32),
    "cpool": ((128, 8), F32),
}


def scratch_specs(T):
    NT = T // 128
    return {
        "xn0_tm": ((T, D), BF16, "A"),
        "q_scr": ((D, T), BF16, "B"),
        "k_scr": ((D, T), BF16, "B"),
        "v_scr": ((NT, 128, 1040), F16, "B"),
        "ho_scr": ((D, T), BF16, "C"),
        "x1_scr": ((T, D), F32, "D"),
        "xn1_tm": ((T, D), BF16, "D"),
        "xm_scr": ((D, T), F32, "E"),
        "u_scr": ((D, T), F32, "E"),
        "z_scr": ((D, T), BF16, "E"),
        "db_scr": ((16, T), F32, "E"),
        "c_scr": ((16, T), F32, "E"),
        "delta_scr": ((1, T), F32, "E"),
        "y2_scr": ((D, T), BF16, "F"),
        "hn_scr": ((HID, T), BF16, "F"),
        "wq16": ((D, D), BF16, "W"),
        "wk16": ((D, D), BF16, "W"),
        "wv16": ((D, D), BF16, "W"),
        "wo16": ((D, D), BF16, "W"),
        "win16": ((D, 2 * D), BF16, "W"),
        "wout16": ((D, D), BF16, "W"),
        "w1m16": ((HB, 128, KB * 128), BF16, "W"),
        "w2m16": ((HB, 128, KB * 128), BF16, "W"),
        "h1m16": ((HB, 128, HB * 128), BF16, "W"),
        "h2m16": ((HB, 128, HB * 128), BF16, "W"),
        "w3T16": ((HID, D), BF16, "W"),
        "wfc16m": ((HB, 128, KB * 128), BF16, "D"),
        "wproj16": ((HID, D), BF16, "D"),
    }


PHASE_WEIGHT_INPUTS = {
    "W": ["wq", "wk", "wv", "wo", "winT", "wout", "w1m", "w2m", "h1m", "h2m", "w3T"],
    "A": [],
    "B": [],
    "C": [],
    "D": ["w1s", "w2s", "h1s", "h2s", "w3srow", "wfcm", "wprojT"],
    "E": ["wxp", "convw", "convb"],
    "F": ["logA", "dparam", "lnw", "lnb"],
    "G": [],
}
PHASE_SCRATCH_IN = {
    "W": [],
    "A": [],
    "B": ["xn0_tm", "wq16", "wk16", "wv16"],
    "C": ["q_scr", "k_scr", "v_scr"],
    "D": ["xn0_tm", "ho_scr", "wo16", "w1m16", "w2m16", "h1m16", "h2m16", "w3T16"],
    "E": ["xn1_tm", "win16"],
    "F": ["u_scr", "z_scr", "db_scr", "c_scr", "delta_scr", "xn1_tm", "wfc16m"],
    "G": ["y2_scr", "hn_scr", "x1_scr", "wout16", "wproj16"],
}
PHASES_ALL = ["W", "A", "B", "C", "D", "E", "F", "G"]


class Kern:
    def __init__(self, T, phases, ctx):
        self.T = T
        self.phases = list(phases)
        self.nc = bass.Bass("TRN2", target_bir_lowering=False)
        self.ctx = ctx
        self.p = Prog(self.nc, ctx)
        self.d = {}
        self.sb = {}
        self.in_names = []
        self.out_names = []
        self._declare_dram()
        self._global_alloc()

    def _declare_dram(self):
        nc, T, ph = self.nc, self.T, set(self.phases)
        need_x = bool({"A", "D"} & ph)
        if need_x:
            self.d["x"] = nc.dram_tensor("x", [T, D], F32, kind="ExternalInput").ap()
            self.in_names.append("x")
        wnames = set()
        for p in self.phases:
            wnames.update(PHASE_WEIGHT_INPUTS[p])
        for nm in sorted(wnames):
            shape, dt = WEIGHT_SPECS[nm]
            self.d[nm] = nc.dram_tensor(nm, list(shape), dt, kind="ExternalInput").ap()
            self.in_names.append(nm)
        for nm, (shape, dt) in CONST_SPECS.items():
            self.d[nm] = nc.dram_tensor(nm, list(shape), dt, kind="ExternalInput").ap()
            self.in_names.append(nm)
        sspec = scratch_specs(T)
        produced = {n for n, (_, _, pp) in sspec.items() if pp in ph}
        consumed = set()
        for p in self.phases:
            consumed.update(PHASE_SCRATCH_IN[p])
        full = set(self.phases) == set(PHASES_ALL)
        for nm, (shape, dt, pp) in sspec.items():
            if pp in ph:
                kind = "Internal" if full else "ExternalOutput"
                self.d[nm] = nc.dram_tensor(nm, list(shape), dt, kind=kind).ap()
                if kind == "ExternalOutput":
                    self.out_names.append(nm)
            elif nm in consumed:
                self.d[nm] = nc.dram_tensor(nm, list(shape), dt, kind="ExternalInput").ap()
                self.in_names.append(nm)
        if "G" in ph:
            self.d["out"] = nc.dram_tensor("out", [T, D], F32, kind="ExternalOutput").ap()
            self.out_names.append("out")

    def _global_alloc(self):
        self.pl = [
            self.ctx.enter_context(self.nc.psum_tensor(f"pl{i}", [128, 1024], F32))
            for i in range(4)
        ]
        self.pb = [self.pl[i // 2][:, (i % 2) * 512:((i % 2) + 1) * 512]
                   for i in range(8)]
        self.sb["onesrow"] = self.ctx.enter_context(
            self.nc.sbuf_tensor("c_onesrow", [1, 128], F32))
        self.sb["onescol"] = self.ctx.enter_context(
            self.nc.sbuf_tensor("c_onescol", [128, 1], F32))
        self.sb["onescol16"] = self.ctx.enter_context(
            self.nc.sbuf_tensor("c_onescol16", [128, 1], BF16))
        self.sb["repmat"] = self.ctx.enter_context(
            self.nc.sbuf_tensor("c_repmat", [16, 16 * 128], F32))
        self.sb["cpool"] = self.ctx.enter_context(
            self.nc.sbuf_tensor("c_cpool", [128, 8], F32))
        self.tok_consts = []
        for nm in ("onesrow", "onescol", "onescol16", "repmat", "cpool"):
            self.tok_consts.append(
                self.p.dma("sync", "consts", self.sb[nm][:, :], self.d[nm]))
        self.tok_consts = [self.tok_consts[-1]]

    def cp(self, i, parts=128):
        return self.sb["cpool"][0:parts, i:i + 1]

    def barrier(self):
        p = self.p
        toks = [("e", e, p.cnt[e]) for e in ENGINES]
        dtoks = [("d", nm, s[1]) for nm, s in p.dslots.items()
                 if not nm.startswith("cast_")]
        for e in ENGINES:
            for t in toks:
                if t[1] != e and t[2] > 0:
                    key = ("e", t[1])
                    if p.waited[e].get(key, -1) < t[2]:
                        p.waited[e][key] = t[2]
                        p.ops[e].append(
                            lambda eng, s=p.esem[t[1]], v=t[2]: eng.wait_ge(s, v))
            for t in dtoks:
                key = ("d", t[1])
                if t[2] > 0 and p.waited[e].get(key, -1) < t[2]:
                    p.waited[e][key] = t[2]
                    p.ops[e].append(
                        lambda eng, s=p.dslots[t[1]][0], v=t[2]: eng.wait_ge(s, v))

    def finalize(self):
        self.barrier()
        self.p.emit()

    from contextlib import contextmanager

    @contextmanager
    def scope(self):
        st = ExitStack()
        kern = self

        class S:
            def sbuf(self, name, shape, dt):
                return st.enter_context(kern.nc.sbuf_tensor(name, shape, dt))

        try:
            yield S()
        finally:
            st.close()
            self.barrier()

    def phase_W(self):
        p = self.p
        self.wtok = {}

        def cast(group, dst, src, pieces=1):
            d, s = self.d[dst], self.d[src]
            n0 = d.shape[0]
            step = n0 // pieces
            tok = None
            for i in range(pieces):
                tok = p.dma("gpsimd", "cast_" + group,
                            d[i * step:(i + 1) * step], s[i * step:(i + 1) * step])
            self.wtok[group] = tok

        cast("qkv", "wq16", "wq")
        cast("qkv", "wk16", "wk")
        cast("qkv", "wv16", "wv")
        self.wtok["qkv"] = self.p.dtok("cast_qkv")
        cast("wo", "wo16", "wo")
        cast("win", "win16", "winT")
        cast("wout", "wout16", "wout")
        cast("w12", "w1m16", "w1m", 2)
        cast("w12", "w2m16", "w2m", 2)
        self.wtok["w12"] = self.p.dtok("cast_w12")
        cast("w3", "w3T16", "w3T")
        cast("h1", "h1m16", "h1m", 4)
        cast("h2", "h2m16", "h2m", 4)

    def wwait(self, group):
        if "W" in self.phases:
            return self.wtok[group]
        return None

    def emit_ln(self, xa, xnb, c, sq, waits_in, extra_apply_waits=()):
        p = self.p
        t_sum = p.op("vector", lambda e, xa=xa, c=c: e.reduce_sum(
            c[:, 0:1], xa[:, :], axis=AX), waits=waits_in)
        t_sq = p.op("scalar", lambda e, xa=xa, s=sq, c=c: e.activation(
            s[:, :], xa[:, :], AF.Square, accum_out=c[:, 1:2]), waits=waits_in)
        t_mean = p.op("vector", lambda e, c=c: e.tensor_scalar(
            c[:, 2:3], c[:, 0:1], 1.0 / D, None, ALU.mult), waits=[t_sum])
        t_m2 = p.op("vector", lambda e, c=c: e.tensor_tensor(
            c[:, 3:4], c[:, 2:3], c[:, 2:3], ALU.mult), waits=[t_mean])
        t_var = p.op("vector", lambda e, c=c: e.scalar_tensor_tensor(
            c[:, 4:5], c[:, 1:2], 1.0 / D, c[:, 3:4], ALU.mult, ALU.subtract),
            waits=[t_sq, t_m2])
        t_ln = p.op("scalar", lambda e, c=c: e.activation(
            c[:, 5:6], c[:, 4:5], AF.Ln, bias=self.cp(0)),
            waits=[t_var] + self.tok_consts)
        t_rstd = p.op("scalar", lambda e, c=c: e.activation(
            c[:, 6:7], c[:, 5:6], AF.Exp, scale=self.cp(1)), waits=[t_ln])
        t_apply = p.op("vector", lambda e, xa=xa, xn=xnb, c=c: e.tensor_scalar(
            xn[:, :], xa[:, :], c[:, 2:3], c[:, 6:7], ALU.subtract, ALU.mult),
            waits=[t_rstd] + list(extra_apply_waits))
        return t_apply, t_sq

    def phase_A(self):
        T, p = self.T, self.p
        NT = T // 128
        with self.scope() as sc:
            xa = [sc.sbuf(f"a_x{i}", [128, D], F32) for i in range(NT)]
            xnb = [sc.sbuf(f"a_xn{j}", [128, D], BF16) for j in range(4)]
            sq = [sc.sbuf(f"a_sq{j}", [128, D], F32) for j in range(2)]
            scol = sc.sbuf("a_scol", [128, NT], F32)
            qcol = sc.sbuf("a_qcol", [128, NT], F32)
            mcol = sc.sbuf("a_mcol", [128, NT], F32)
            vcol = sc.sbuf("a_vcol", [128, NT], F32)
            rcol = sc.sbuf("a_rcol", [128, NT], F32)
            GRP = max(1, NT // 4)
            loads = []
            for i in range(NT):
                loads.append(p.dma("sync", f"a_xl{i // GRP}", xa[i][:, :],
                                   self.d["x"][i * 128:(i + 1) * 128, :]))
            gtok = [loads[min((g + 1) * GRP - 1, NT - 1)]
                    for g in range(NT // GRP)]
            t_r = None
            t_s = None
            sq_tok = [None, None]
            for i in range(NT):
                t_r = p.op("vector", lambda e, i=i: e.reduce_sum(
                    scol[:, i:i + 1], xa[i][:, :], axis=AX),
                    waits=[gtok[i // GRP]])
                t_s = p.op("scalar", lambda e, i=i: e.activation(
                    sq[i % 2][:, :], xa[i][:, :], AF.Square,
                    accum_out=qcol[:, i:i + 1]),
                    waits=[gtok[i // GRP], sq_tok[i % 2]])
                sq_tok[i % 2] = t_s
            t1 = p.op("vector", lambda e: e.tensor_scalar(
                mcol[:, :], scol[:, :], 1.0 / D, None, ALU.mult), waits=[t_r])
            t2 = p.op("vector", lambda e: e.tensor_tensor(
                vcol[:, :], mcol[:, :], mcol[:, :], ALU.mult), waits=[t1])
            t3 = p.op("vector", lambda e: e.scalar_tensor_tensor(
                vcol[:, :], qcol[:, :], 1.0 / D, vcol[:, :],
                ALU.mult, ALU.subtract), waits=[t_s, t2])
            t4 = p.op("scalar", lambda e: e.activation(
                vcol[:, :], vcol[:, :], AF.Ln, bias=self.cp(0)),
                waits=[t3] + self.tok_consts)
            t5 = p.op("scalar", lambda e: e.activation(
                rcol[:, :], vcol[:, :], AF.Exp, scale=self.cp(1)), waits=[t4])
            st_tok = [None] * 4
            for i in range(NT):
                j = i % 4
                t_a = p.op("vector", lambda e, i=i, j=j: e.tensor_scalar(
                    xnb[j][:, :], xa[i][:, :], mcol[:, i:i + 1],
                    rcol[:, i:i + 1], ALU.subtract, ALU.mult),
                    waits=[t5, st_tok[j]])
                st_tok[j] = p.dma(
                    "sync", f"a_xn{j}",
                    self.d["xn0_tm"][i * 128:(i + 1) * 128, :], xnb[j][:, :],
                    waits=[t_a])

    def phase_B(self):
        T, p = self.T, self.p
        QT = min(1024, T)
        NQ = T // QT
        NHF = QT // 512
        with self.scope() as sc:
            xnf = [sc.sbuf(f"b_xnf{kb}", [128, QT], BF16) for kb in range(KB)]
            wsb = [[sc.sbuf(f"b_w{j}_{kb}", [128, D], BF16) for kb in range(KB)]
                   for j in range(2)]
            qst = [sc.sbuf(f"b_qst{j}", [128, QT], BF16) for j in range(2)]
            vst = [sc.sbuf(f"b_vst{j}", [128, 1040], F16) for j in range(2)]

            wload = [None, None]
            xnf_consumed = None

            def load_weights(j, name16, waits):
                tok = None
                for kb in range(KB):
                    tok = p.dma("sync", f"b_w{j}", wsb[j][kb][:, :],
                                self.d[name16][kb * 128:(kb + 1) * 128, :],
                                waits=waits)
                return tok

            st_tok = [None, None]
            ev_tok = [None, None, None, None]
            vev_tok = [None, None]
            vst_store = [None, None]

            for ipass, (nm, w16) in enumerate(
                    (("q", "wq16"), ("k", "wk16"), ("v", "wv16"))):
                wj = ipass % 2
                tw = load_weights(wj, w16, [self.wwait("qkv"), wload[wj]])
                for q in range(NQ):
                    txn = None
                    for kb in range(KB):
                        txn = p.dma(
                            "sync", "b_xnf", xnf[kb][:, :],
                            self.d["xn0_tm"][q * QT:(q + 1) * QT,
                                             kb * 128:(kb + 1) * 128],
                            waits=[xnf_consumed], transpose=True)
                    if nm in ("q", "k"):
                        for m in range(KB):
                            for hf in range(NHF):
                                bank = 2 * (m % 2) + hf
                                def mmfn(e, m=m, hf=hf, bank=bank, wj=wj):
                                    r = None
                                    for kb in range(KB):
                                        r = e.matmul(
                                            self.pb[bank][:, :],
                                            wsb[wj][kb][:, m * 128:(m + 1) * 128],
                                            xnf[kb][:, hf * 512:(hf + 1) * 512],
                                            start=(kb == 0), stop=(kb == KB - 1))
                                    return r
                                tmm = p.op("tensor", mmfn,
                                           waits=[tw, txn, ev_tok[bank]])
                                def evfn(e, bank=bank, hf=hf, mj=m % 2):
                                    return e.copy(
                                        qst[mj][:, hf * 512:(hf + 1) * 512],
                                        self.pb[bank][:, :])
                                ev_tok[bank] = p.op(
                                    "scalar", evfn,
                                    waits=[tmm, st_tok[m % 2]])
                            dst = self.d["q_scr" if nm == "q" else "k_scr"]
                            st_tok[m % 2] = p.dma(
                                "sync", f"b_{nm}st{m % 2}",
                                dst[m * 128:(m + 1) * 128, q * QT:(q + 1) * QT],
                                qst[m % 2][:, :],
                                waits=[ev_tok[2 * (m % 2)], ev_tok[2 * (m % 2) + 1]])
                        xnf_consumed = p.tok("tensor")
                    else:
                        for tt in range(QT // 128):
                            bank0, bank1 = 4 + 2 * (tt % 2), 5 + 2 * (tt % 2)
                            def mmv(e, tt=tt, bank0=bank0, bank1=bank1, wj=wj):
                                r = None
                                for kb in range(KB):
                                    e.matmul(self.pb[bank0][:, :],
                                             xnf[kb][:, tt * 128:(tt + 1) * 128],
                                             wsb[wj][kb][:, 0:512],
                                             start=(kb == 0), stop=(kb == KB - 1))
                                    r = e.matmul(self.pb[bank1][:, :],
                                                 xnf[kb][:, tt * 128:(tt + 1) * 128],
                                                 wsb[wj][kb][:, 512:1024],
                                                 start=(kb == 0), stop=(kb == KB - 1))
                                return r
                            tmm = p.op("tensor", mmv,
                                       waits=[tw, txn, vev_tok[tt % 2]])
                            def evv(e, tt=tt, bank0=bank0, bank1=bank1):
                                v3 = vst[tt % 2][:, :].rearrange(
                                    "p (h c) -> p h c", c=65)
                                e.tensor_copy(
                                    v3[:, 0:8, 0:64],
                                    self.pb[bank0][:, :].rearrange(
                                        "p (h c) -> p h c", c=64))
                                e.tensor_copy(
                                    v3[:, 8:16, 0:64],
                                    self.pb[bank1][:, :].rearrange(
                                        "p (h c) -> p h c", c=64))
                                return e.memset(v3[:, :, 64:65], 1.0)
                            tev = p.op("vector", evv,
                                       waits=[tmm, vst_store[tt % 2]])
                            vev_tok[tt % 2] = tev
                            vst_store[tt % 2] = p.dma(
                                "sync", f"b_vst{tt % 2}",
                                self.d["v_scr"][q * (QT // 128) + tt],
                                vst[tt % 2][:, :], waits=[tev])
                        xnf_consumed = p.tok("tensor")
                wload[wj] = p.tok("tensor")

    def phase_C(self):
        T, p = self.T, self.p
        NTB = T // 128
        NSC = T // 512
        with self.scope() as sc:
            kpb = [sc.sbuf(f"c_k{j}", [128, T], BF16) for j in range(2)]
            vsb = [sc.sbuf(f"c_v{i}", [128, 1040], F16) for i in range(NTB)]
            qch = [sc.sbuf(f"c_q{j}", [128, 512], BF16) for j in range(2)]
            attAB = [sc.sbuf(f"c_at{i}", [128, 1024], F16) for i in range(NTB)]
            rden = sc.sbuf("c_rden", [1, 1024], F32)
            bcf = [sc.sbuf(f"c_bcf{x}", [64, 512], F32) for x in range(2)]
            hout = [sc.sbuf(f"c_ho{x}", [64, 512], BF16) for x in range(2)]

            tok_v = None
            for i in range(NTB):
                tok_v = p.dma("sync", "c_vld", vsb[i][:, :], self.d["v_scr"][i])

            PLS = (0, 1, 3)
            exp_tok = {}
            av_tok = {}
            t_nm_prev = [None, None]
            norm_guard = None
            kcons = [None, None]
            qcons = [None, None]
            ho_store = [None, None]
            t_bc_prev = [None, None]
            t_cp_prev = [None, None]
            for hp in range(NHEAD // 2):
                jk = hp % 2
                tk = p.dma("sync", "c_kld", kpb[jk][:, :],
                           self.d["k_scr"][hp * 128:(hp + 1) * 128, :],
                           waits=[kcons[jk]])
                for sci in range(NSC):
                    jq = sci % 2
                    tq = p.dma("sync", "c_qld", qch[jq][:, :],
                               self.d["q_scr"][hp * 128:(hp + 1) * 128,
                                               sci * 512:(sci + 1) * 512],
                               waits=[qcons[jq]])

                    def emit_scores(tb):
                        r = PLS[tb % 3]
                        w = [tk, tq, exp_tok.get((tb - 3) % NTB)]
                        if r == 3:
                            w.append(norm_guard)
                        p.op("tensor", lambda e, tb=tb, r=r, jk=jk, jq=jq:
                             e.matmul(self.pl[r][:, 0:512],
                                      kpb[jk][0:64, tb * 128:(tb + 1) * 128],
                                      qch[jq][0:64, :],
                                      start=True, stop=True),
                             waits=w, inc=False)
                        t_s = p.op("tensor", lambda e, tb=tb, r=r, jk=jk, jq=jq:
                                   e.matmul(self.pl[r][:, 512:1024],
                                            kpb[jk][64:128,
                                                    tb * 128:(tb + 1) * 128],
                                            qch[jq][64:128, :],
                                            start=True, stop=True),
                                   waits=[])
                        exp_tok[tb] = p.op(
                            "scalar", lambda e, tb=tb, r=r:
                            e.activation(attAB[tb][:, :], self.pl[r][:, :],
                                         AF.Exp, scale=self.cp(2)),
                            waits=[t_s, av_tok.get(tb)] + self.tok_consts)

                    def emit_av(tb):
                        w = [exp_tok[tb], tok_v]
                        if tb == 0:
                            w += t_nm_prev
                        hA, hB = 2 * hp, 2 * hp + 1
                        p.op("tensor", lambda e, tb=tb, hA=hA:
                             e.matmul(self.pb[4][0:65, :],
                                      vsb[tb][:, 65 * hA:65 * hA + 65],
                                      attAB[tb][:, 0:512],
                                      start=(tb == 0), stop=(tb == NTB - 1)),
                             waits=w, inc=False)
                        t = p.op("tensor", lambda e, tb=tb, hB=hB:
                                 e.matmul(self.pb[5][0:65, :],
                                          vsb[tb][:, 65 * hB:65 * hB + 65],
                                          attAB[tb][:, 512:1024],
                                          start=(tb == 0), stop=(tb == NTB - 1)),
                                 waits=[])
                        av_tok[tb] = t
                        return t

                    last = None
                    for tb in range(NTB):
                        emit_scores(tb)
                        if tb >= 2:
                            last = emit_av(tb - 2)
                    last = emit_av(NTB - 2)
                    last = emit_av(NTB - 1)

                    for x, (avbank, bcbank) in enumerate(((4, 6), (5, 7))):
                        h = 2 * hp + x
                        t_r0 = p.op("scalar", lambda e, x=x, avbank=avbank:
                                    e.activation(rden[0:1, x * 512:x * 512 + 512],
                                                 self.pb[avbank][64:65, :],
                                                 AF.Ln),
                                    waits=[last, t_bc_prev[x]])
                        t_r = p.op("scalar", lambda e, x=x: e.activation(
                            rden[0:1, x * 512:x * 512 + 512],
                            rden[0:1, x * 512:x * 512 + 512],
                            AF.Exp, scale=self.cp(3, 1)),
                            waits=[t_r0] + self.tok_consts)
                        t_bc = p.op("tensor", lambda e, x=x, bcbank=bcbank:
                                    e.matmul(self.pb[bcbank][0:64, :],
                                             self.sb["onesrow"][0:1, 0:64],
                                             rden[0:1, x * 512:x * 512 + 512],
                                             start=True, stop=True),
                                    waits=[t_r, t_cp_prev[x]] + self.tok_consts)
                        t_bc_prev[x] = t_bc
                        t_cp = p.op("scalar", lambda e, x=x, bcbank=bcbank:
                                    e.copy(bcf[x][:, :], self.pb[bcbank][0:64, :]),
                                    waits=[t_bc, ho_store[x], t_nm_prev[x]])
                        t_cp_prev[x] = t_cp
                        norm_guard = t_cp
                        t_nm = p.op("vector", lambda e, x=x, avbank=avbank:
                                    e.tensor_tensor(hout[x][:, :],
                                                    self.pb[avbank][0:64, :],
                                                    bcf[x][:, :], ALU.mult),
                                    waits=[t_cp, ho_store[x]])
                        t_nm_prev[x] = t_nm
                        ho_store[x] = p.dma(
                            "sync", f"c_host{x}",
                            self.d["ho_scr"][h * 64:(h + 1) * 64,
                                             sci * 512:(sci + 1) * 512],
                            hout[x][:, :], waits=[t_nm])
                    qcons[jq] = p.tok("tensor")
                kcons[jk] = p.tok("tensor")

    def _emit_ternarize(self, src_name, dst_name, nblk, sc, guard=None,
                        bufs=None, prev=None):
        p = self.p
        if True:
            if bufs is None:
                blk = [sc.sbuf("tz_b0", [128, KB * 128], F32)]
                outb = [sc.sbuf("tz_o0", [128, KB * 128], BF16)]
            else:
                blk, outb = bufs
            self._tz_bufs = (blk, outb)
            acc = sc.sbuf(f"tz_acc{src_name}", [128, 4], F32)
            gcol = sc.sbuf(f"tz_g{src_name}", [128, 1], F32)
            grow = sc.sbuf(f"tz_gr{src_name}", [1, 4], F32)
            src = self.d[src_name]
            is3d = len(src.shape) == 3
            t_acc = None
            rd_tok = [prev, None]
            for i in range(nblk):
                j = 0
                tl = p.dma("gpsimd", f"tz_l{src_name}{j}", blk[j][:, :],
                           src[i] if is3d else src[i * 128:(i + 1) * 128, :],
                           waits=[rd_tok[j]])
                t_r = p.op("vector", lambda e, j=j: e.reduce_sum(
                    acc[:, 1:2], blk[j][:, :], axis=AX,
                    apply_absolute_value=True), waits=[tl, t_acc])
                if i == 0:
                    t_acc = p.op("vector", lambda e: e.tensor_copy(
                        acc[:, 0:1], acc[:, 1:2]), waits=[t_r])
                else:
                    t_acc = p.op("vector", lambda e: e.tensor_tensor(
                        acc[:, 0:1], acc[:, 0:1], acc[:, 1:2], ALU.add),
                        waits=[t_r])
                rd_tok[j] = t_r
            t_ps = p.op("tensor", lambda e: e.matmul(
                self.pb[6][0:1, 0:1], acc[:, 0:1], self.sb["onescol"][:, 0:1],
                start=True, stop=True),
                waits=[t_acc, guard] + self.tok_consts)
            nelem = float(nblk * 128 * KB * 128)
            t_g = p.op("vector", lambda e: e.tensor_scalar(
                grow[0:1, 0:1], self.pb[6][0:1, 0:1], 1.0 / nelem, 1e-6,
                ALU.mult, ALU.add), waits=[t_ps])
            t_gr = p.op("vector", lambda e: e.reciprocal(
                grow[0:1, 1:2], grow[0:1, 0:1]), waits=[t_g])
            t_bc = p.op("tensor", lambda e: e.matmul(
                self.pb[6][0:128, 0:1], self.sb["onesrow"][0:1, :],
                grow[0:1, 1:2], start=True, stop=True), waits=[t_gr, t_g])
            t_gc = p.op("scalar", lambda e: e.copy(
                gcol[:, 0:1], self.pb[6][0:128, 0:1]), waits=[t_bc])
            st_tok = [getattr(self, "_tz_store_prev", None)
                      if bufs is not None else None, None]
            wr_tok = [None, None]
            for i in range(nblk):
                j = 0
                tl = p.dma("gpsimd", f"tz_l{src_name}{j}", blk[j][:, :],
                           src[i] if is3d else src[i * 128:(i + 1) * 128, :],
                           waits=[rd_tok[j]])
                t1 = p.op("vector", lambda e, j=j: e.tensor_scalar(
                    blk[j][:, :], blk[j][:, :], gcol[:, 0:1], MAGIC,
                    ALU.mult, ALU.add), waits=[tl, t_gc, wr_tok[j]])
                t2 = p.op("vector", lambda e, j=j: e.tensor_scalar(
                    blk[j][:, :], blk[j][:, :], MAGIC, 1.0,
                    ALU.subtract, ALU.min), waits=[t1])
                t3 = p.op("vector", lambda e, j=j: e.tensor_scalar(
                    outb[j][:, :], blk[j][:, :], -1.0, None,
                    ALU.max), waits=[t2, st_tok[j]])
                rd_tok[j] = t3
                wr_tok[j] = t3
                dst = self.d[dst_name]
                st_tok[j] = p.dma("gpsimd", f"tz_s{src_name}{j}",
                                  dst[i] if len(dst.shape) == 3
                                  else dst[i * 128:(i + 1) * 128, :],
                                  outb[j][:, :], waits=[t3])
            self._tz_prev = t3
            self._tz_store_prev = st_tok[0]
            return t_gc

    def phase_D(self):
        T, p = self.T, self.p
        NCH = T // 512
        with self.scope() as sc:
            xnf = [sc.sbuf(f"d_xnf{i}", [128, 512], BF16) for i in range(KB)]
            hof = [sc.sbuf(f"d_hof{i}", [128, 512], BF16) for i in range(KB)]
            w1b = [sc.sbuf(f"d_w1b{j}", [128, KB * 128], BF16) for j in range(2)]
            w2b = [sc.sbuf(f"d_w2b{j}", [128, KB * 128], BF16) for j in range(2)]
            whb = [sc.sbuf(f"d_whb{j}", [128, HB * 128], BF16) for j in range(2)]
            hin = [sc.sbuf(f"d_hin{m}", [128, 512], BF16) for m in range(HB)]
            hmid = [sc.sbuf(f"d_hmid{m}", [128, 512], BF16) for m in range(HB)]
            w3sb = [sc.sbuf(f"d_w3_{i}", [128, D], BF16) for i in range(HB // 4)]
            wosb = [sc.sbuf(f"d_wo{i}", [128, D], BF16) for i in range(KB)]
            tmpA = [sc.sbuf(f"d_tmpA{j}", [128, 512], F32) for j in range(2)]
            g0t = [sc.sbuf(f"d_g0_{i}", [128, 512], F32) for i in range(8)]
            s0sl = sc.sbuf("d_s0sl", [128, D], F32)
            sgt = sc.sbuf("d_sg", [128, D], F32)
            x1sl = sc.sbuf("d_x1sl", [128, D], F32)
            xsl = [sc.sbuf(f"d_xsl{j}", [128, D], F32) for j in range(2)]
            xn1sl = sc.sbuf("d_xn1sl", [128, D], BF16)
            w3sbc = sc.sbuf("d_w3sbc", [128, D], F32)
            w3srow = sc.sbuf("d_w3srow", [1, D], F32)
            scl = {}
            for nm in ("w1s", "w2s", "h1s", "h2s"):
                scl[nm] = sc.sbuf("d_" + nm, [128, HB], F32)
            sq = sc.sbuf("d_sq", [128, D], F32)
            cols = sc.sbuf("d_cols", [128, 8], F32)

            g_tz1 = self._emit_ternarize("wfcm", "wfc16m", HB, sc)
            g_tz2 = self._emit_ternarize("wprojT", "wproj16", HB, sc,
                                         guard=g_tz1, bufs=self._tz_bufs,
                                         prev=self._tz_prev)

            t_scl = None
            for nm in ("w1s", "w2s", "h1s", "h2s"):
                t_scl = p.dma("sync", "d_scl", scl[nm][:, :], self.d[nm])
            t_w3r = p.dma("sync", "d_scl", w3srow[:, :], self.d["w3srow"])
            t_scl = self.p.dtok("d_scl")
            t_wo = None
            for i in range(KB):
                t_wo = p.dma("sync", "d_wold", wosb[i][:, :],
                             self.d["wo16"][i * 128:(i + 1) * 128, :],
                             waits=[self.wwait("wo")])
            t_b1 = p.op("tensor", lambda e: e.matmul(
                self.pb[0][:, :], self.sb["onesrow"][0:1, :],
                w3srow[0:1, 0:512], start=True, stop=True),
                waits=[t_w3r] + self.tok_consts)
            t_b2 = p.op("tensor", lambda e: e.matmul(
                self.pb[1][:, :], self.sb["onesrow"][0:1, :],
                w3srow[0:1, 512:1024], start=True, stop=True), waits=[t_w3r])
            t_c1 = p.op("scalar", lambda e: e.copy(
                w3sbc[:, 0:512], self.pb[0][:, :]), waits=[t_b1])
            t_c2 = p.op("scalar", lambda e: e.copy(
                w3sbc[:, 512:1024], self.pb[1][:, :]), waits=[t_b2])
            tok_w3sbc = t_c2

            xnf_cons = None
            hof_cons = None
            w1l_tok = [None, None]
            w2l_tok = [None, None]
            whl_tok = [None, None]
            w3l_tok = None
            w3_cons = None
            ev_hin = {}
            ev_hmid = {}
            ev_hout = {}
            x1_store = None
            xn1_store = None
            lnapply_tok = None
            lnsq_tok = None
            prev_guard = []
            tz_guard = [g_tz1, g_tz2]
            prev_s0 = {}
            t_sg_prev = None
            t_x1_prev = None
            for c in range(NCH):
                txn = None
                for i in range(KB):
                    txn = p.dma("sync", "d_xnfl", xnf[i][:, :],
                                self.d["xn0_tm"][c * 512:(c + 1) * 512,
                                                 i * 128:(i + 1) * 128],
                                waits=[xnf_cons], transpose=True)
                tho = None
                for i in range(KB):
                    tho = p.dma("sync", "d_hofl", hof[i][:, :],
                                self.d["ho_scr"][i * 128:(i + 1) * 128,
                                                 c * 512:(c + 1) * 512],
                                waits=[hof_cons])
                for m in range(HB):
                    j = m % 2
                    tl1 = p.dma("sync", f"d_w1l{j}", w1b[j][:, :],
                                self.d["w1m16"][m], waits=[w1l_tok[j],
                                                           self.wwait("w12")])
                    tl2 = p.dma("sync", f"d_w2l{j}", w2b[j][:, :],
                                self.d["w2m16"][m], waits=[w2l_tok[j],
                                                           self.wwait("w12")])
                    bank1, bank2 = j, 2 + j

                    def mm1(e, j=j, bank1=bank1):
                        r = None
                        for kb in range(KB):
                            r = e.matmul(self.pb[bank1][:, :],
                                         w1b[j][:, kb * 128:(kb + 1) * 128],
                                         xnf[kb][:, :],
                                         start=(kb == 0), stop=(kb == KB - 1))
                        return r

                    def mm2(e, j=j, bank2=bank2):
                        r = None
                        for kb in range(KB):
                            r = e.matmul(self.pb[bank2][:, :],
                                         w2b[j][:, kb * 128:(kb + 1) * 128],
                                         xnf[kb][:, :],
                                         start=(kb == 0), stop=(kb == KB - 1))
                        return r

                    wguard = [ev_hin.get((m - 2) % HB)]
                    if m < 2:
                        wguard += [tok_w3sbc] + prev_guard
                    tm1 = p.op("tensor", mm1, waits=[tl1, txn] + wguard)
                    tm2 = p.op("tensor", mm2, waits=[tl2] + wguard)
                    w1l_tok[j] = tm1
                    w2l_tok[j] = tm2
                    te1 = p.op("scalar", lambda e, j=j, m=m, bank1=bank1:
                               e.activation(tmpA[j][:, :], self.pb[bank1][:, :],
                                            AF.Silu, scale=scl["w1s"][:, m:m + 1]),
                               waits=[tm1, t_scl, ev_hin.get((m - 2) % HB)])
                    ev_hin[m] = p.op("vector", lambda e, j=j, m=m, bank2=bank2:
                                     e.scalar_tensor_tensor(
                                         hin[m][:, :], self.pb[bank2][:, :],
                                         scl["w2s"][:, m:m + 1], tmpA[j][:, :],
                                         ALU.mult, ALU.mult),
                                     waits=[tm2, te1])
                xnf_cons = p.tok("tensor")
                for m in range(HB):
                    j = m % 2
                    tlh = p.dma("sync", f"d_whl{j}", whb[j][:, :],
                                self.d["h1m16"][m], waits=[whl_tok[j],
                                                           self.wwait("h1")])
                    bank = 4 + j

                    def mmh(e, j=j, bank=bank):
                        r = None
                        for kb in range(HB):
                            r = e.matmul(self.pb[bank][:, :],
                                         whb[j][:, kb * 128:(kb + 1) * 128],
                                         hin[kb][:, :],
                                         start=(kb == 0), stop=(kb == HB - 1))
                        return r

                    w = [tlh, ev_hmid.get((m - 2) % HB)]
                    if m == 0:
                        w.append(ev_hin[HB - 1])
                    tmh = p.op("tensor", mmh, waits=w)
                    whl_tok[j] = tmh
                    ev_hmid[m] = p.op("scalar", lambda e, m=m, bank=bank:
                                      e.activation(hmid[m][:, :],
                                                   self.pb[bank][:, :], AF.Gelu,
                                                   scale=scl["h1s"][:, m:m + 1]),
                                      waits=[tmh])
                for m in range(HB):
                    j = m % 2
                    tlh = p.dma("sync", f"d_whl{j}", whb[j][:, :],
                                self.d["h2m16"][m], waits=[whl_tok[j],
                                                           self.wwait("h2")])
                    bank = 4 + j

                    def mmh2(e, j=j, bank=bank):
                        r = None
                        for kb in range(HB):
                            r = e.matmul(self.pb[bank][:, :],
                                         whb[j][:, kb * 128:(kb + 1) * 128],
                                         hmid[kb][:, :],
                                         start=(kb == 0), stop=(kb == HB - 1))
                        return r

                    w = [tlh, ev_hout.get((m - 2) % HB)]
                    if m == 0:
                        w.append(ev_hmid[HB - 1])
                    if m == 0:
                        w.append(whl_tok[(HB - 1) % 2])
                    tmh = p.op("tensor", mmh2, waits=w)
                    whl_tok[j] = tmh
                    ev_hout[m] = p.op("scalar", lambda e, m=m, bank=bank:
                                      e.activation(hin[m][:, :],
                                                   self.pb[bank][:, :], AF.Gelu,
                                                   scale=scl["h2s"][:, m:m + 1]),
                                      waits=[tmh])
                for qtr in range(4):
                    tl3 = None
                    for i in range(HB // 4):
                        tl3 = p.dma("sync", "d_w3l", w3sb[i][:, :],
                                    self.d["w3T16"][(qtr * 8 + i) * 128:
                                                    (qtr * 8 + i + 1) * 128, :],
                                    waits=[w3_cons, self.wwait("w3")])
                    w3l_tok = tl3

                    def mm3(e, qtr=qtr):
                        r = None
                        for ts in range(4):
                            for hf in range(2):
                                bank = ts * 2 + hf
                                for i in range(HB // 4):
                                    kb = qtr * 8 + i
                                    r = e.matmul(
                                        self.pb[bank][:, :],
                                        hin[kb][:, ts * 128:(ts + 1) * 128],
                                        w3sb[i][:, hf * 512:(hf + 1) * 512],
                                        start=(kb == 0), stop=(kb == HB - 1))
                        return r

                    w = [w3l_tok]
                    if qtr == 0:
                        w += [ev_hout[mm] for mm in range(HB)]
                        w.append(tok_w3sbc)
                        w += prev_guard
                        if c == 0:
                            w += tz_guard
                    tm3 = p.op("tensor", mm3, waits=w)
                    w3_cons = tm3
                ev_g0 = {}
                for ts in range(4):
                    for hf in range(2):
                        bank = ts * 2 + hf
                        tv = p.op("vector", lambda e, bank=bank, hf=hf:
                                  e.tensor_tensor(
                                      g0t[bank][:, :], self.pb[bank][:, :],
                                      w3sbc[:, hf * 512:(hf + 1) * 512],
                                      ALU.mult),
                                  waits=[w3_cons, tok_w3sbc] + prev_guard)
                        ev_g0[bank] = p.op("scalar", lambda e, bank=bank:
                                           e.activation(g0t[bank][:, :],
                                                        g0t[bank][:, :], AF.Silu),
                                           waits=[tv])
                g0v_all = p.tok("vector")
                for ts in range(4):
                    bank1, bank2 = ts % 2, 2 + ts % 2

                    def mmo(e, ts=ts, bank1=bank1, bank2=bank2):
                        r = None
                        for fb in range(KB):
                            e.matmul(self.pb[bank1][:, :],
                                     hof[fb][:, ts * 128:(ts + 1) * 128],
                                     wosb[fb][:, 0:512],
                                     start=(fb == 0), stop=(fb == KB - 1))
                            r = e.matmul(self.pb[bank2][:, :],
                                         hof[fb][:, ts * 128:(ts + 1) * 128],
                                         wosb[fb][:, 512:1024],
                                         start=(fb == 0), stop=(fb == KB - 1))
                        return r

                    w = [tho, t_wo, g0v_all, prev_s0.get((ts - 2) % 4)]
                    tmo = p.op("tensor", mmo, waits=w)
                    t_s0 = p.op("scalar", lambda e, bank1=bank1, bank2=bank2:
                                (e.copy(s0sl[:, 0:512], self.pb[bank1][:, :]),
                                 e.copy(s0sl[:, 512:1024],
                                        self.pb[bank2][:, :]))[-1],
                                waits=[tmo, t_sg_prev])
                    prev_s0[ts] = t_s0
                    t_sg = p.op("vector", lambda e, ts=ts: (
                        e.tensor_tensor(sgt[:, 0:512], s0sl[:, 0:512],
                                        g0t[ts * 2][:, :], ALU.mult),
                        e.tensor_tensor(sgt[:, 512:1024], s0sl[:, 512:1024],
                                        g0t[ts * 2 + 1][:, :], ALU.mult))[-1],
                        waits=[t_s0, ev_g0[ts * 2], ev_g0[ts * 2 + 1],
                               t_x1_prev])
                    t_sg_prev = t_sg
                    tx = p.dma("sync", f"d_xl{ts % 2}", xsl[ts % 2][:, :],
                               self.d["x"][c * 512 + ts * 128:
                                           c * 512 + (ts + 1) * 128, :],
                               waits=[lnapply_tok, lnsq_tok])
                    t_x1 = p.op("vector", lambda e, ts=ts: e.tensor_tensor(
                        x1sl[:, :], sgt[:, :], xsl[ts % 2][:, :], ALU.add),
                        waits=[t_sg, tx, x1_store, lnapply_tok])
                    t_x1_prev = t_x1
                    x1_store = p.dma("sync", "d_x1st",
                                     self.d["x1_scr"][c * 512 + ts * 128:
                                                      c * 512 + (ts + 1) * 128, :],
                                     x1sl[:, :], waits=[t_x1])
                    lnapply_tok, lnsq_tok = self.emit_ln(
                        x1sl, xn1sl, cols, sq, [t_x1, lnsq_tok],
                        extra_apply_waits=[xn1_store])
                    xn1_store = p.dma("sync", "d_xn1st",
                                      self.d["xn1_tm"][c * 512 + ts * 128:
                                                       c * 512 + (ts + 1) * 128, :],
                                      xn1sl[:, :], waits=[lnapply_tok])
                hof_cons = p.tok("tensor")
                prev_guard = [p.tok("vector"), p.tok("scalar")]

    def phase_E(self):
        T, p = self.T, self.p
        NCH = T // 512
        with self.scope() as sc:
            xnf = [sc.sbuf(f"e_xnf{i}", [128, 512], BF16) for i in range(KB)]
            winsb = [sc.sbuf(f"e_win{i}", [128, 2 * D], BF16) for i in range(KB)]
            xmst = [sc.sbuf(f"e_xm{j}", [128, 512], F32) for j in range(2)]
            zst = [sc.sbuf(f"e_z{j}", [128, 512], BF16) for j in range(2)]
            xmc = [sc.sbuf(f"e_xmc{j}", [128, 515], F32) for j in range(2)]
            cacc = [sc.sbuf(f"e_cacc{j}", [128, 512], F32) for j in range(2)]
            ucv = [sc.sbuf(f"e_ucv{i}", [128, 512], F32) for i in range(KB)]
            wxpsb = [sc.sbuf(f"e_wxp{i}", [128, 65], F32) for i in range(KB)]
            convw = sc.sbuf("e_convw", [128, KB * 4], F32)
            convb = sc.sbuf("e_convb", [128, KB], F32)
            prow = sc.sbuf("e_prow", [1, 512], F32)
            drow = sc.sbuf("e_drow", [1, 4 * 512], F32)
            bsb = sc.sbuf("e_bsb", [16, 512], F32)
            dbb = sc.sbuf("e_dbb", [16, 512], F32)
            dbc16 = sc.sbuf("e_dbc16", [16, 512], F32)
            csb = sc.sbuf("e_csb", [16, 512], F32)

            t_wl = None
            for i in range(KB):
                t_wl = p.dma("sync", "e_winl", winsb[i][:, :],
                             self.d["win16"][i * 128:(i + 1) * 128, :],
                             waits=[self.wwait("win")])
            t_cw = p.dma("sync", "e_cst", convw[:, :], self.d["convw"])
            t_cb = p.dma("sync", "e_cst", convb[:, :], self.d["convb"])
            t_xp = None
            for i in range(KB):
                t_xp = p.dma("sync", "e_cst", wxpsb[i][:, :],
                             self.d["wxp"][i * 128:(i + 1) * 128, :])
            tok_cst = self.p.dtok("e_cst")

            xnf_cons = None
            xm_store = [None, None]
            z_store = [None, None]
            ev_tok = {}
            xmc_tok = [None, None]
            cacc_tok = [None, None]
            u_st_tok = {}
            xm_ready_chunk = []
            for c in range(NCH):
                txn = None
                for i in range(KB):
                    txn = p.dma("sync", "e_xnfl", xnf[i][:, :],
                                self.d["xn1_tm"][c * 512:(c + 1) * 512,
                                                 i * 128:(i + 1) * 128],
                                waits=[xnf_cons], transpose=True)
                for m in range(2 * KB):
                    j = m % 2
                    bank = j

                    def mmw(e, m=m, bank=bank):
                        r = None
                        for kb in range(KB):
                            r = e.matmul(self.pb[bank][:, :],
                                         winsb[kb][:, m * 128:(m + 1) * 128],
                                         xnf[kb][:, :],
                                         start=(kb == 0), stop=(kb == KB - 1))
                        return r

                    tmm = p.op("tensor", mmw,
                               waits=[t_wl, txn, ev_tok.get((m - 2) % (2 * KB))])
                    if m < KB:
                        ev = p.op("scalar", lambda e, j=j, bank=bank: e.copy(
                            xmst[j][:, :], self.pb[bank][:, :]),
                            waits=[tmm, xm_store[j]])
                        xm_store[j] = p.dma(
                            "sync", f"e_xmst{j}",
                            self.d["xm_scr"][m * 128:(m + 1) * 128,
                                             c * 512:(c + 1) * 512],
                            xmst[j][:, :], waits=[ev])
                    else:
                        ev = p.op("scalar", lambda e, j=j, bank=bank: e.copy(
                            zst[j][:, :], self.pb[bank][:, :]),
                            waits=[tmm, z_store[j]])
                        z_store[j] = p.dma(
                            "sync", f"e_zst{j}",
                            self.d["z_scr"][(m - KB) * 128:(m - KB + 1) * 128,
                                            c * 512:(c + 1) * 512],
                            zst[j][:, :], waits=[ev])
                    ev_tok[m] = ev
                xnf_cons = p.tok("tensor")
                xm_ready_chunk.append((self.p.dtok("e_xmst0"),
                                       self.p.dtok("e_xmst1")))

            ucv_cons = None
            prev_pp = None
            t_db_prev = None
            t_csb_prev = None
            dst_tok = None
            dbst_tok = None
            cst_tok = None
            for c in range(NCH):
                w_halo = [xm_ready_chunk[c][0], xm_ready_chunk[c][1]]
                for i in range(KB):
                    j = i % 2
                    if c == 0:
                        t_ld = p.dma("sync", "e_xmcl", xmc[j][:, 3:515],
                                     self.d["xm_scr"][i * 128:(i + 1) * 128,
                                                      0:512],
                                     waits=w_halo + [xmc_tok[j]])
                        t_z3 = p.op("vector", lambda e, j=j: e.memset(
                            xmc[j][:, 0:3], 0.0), waits=[xmc_tok[j]])
                    else:
                        t_ld = p.dma("sync", "e_xmcl", xmc[j][:, 0:515],
                                     self.d["xm_scr"][i * 128:(i + 1) * 128,
                                                      c * 512 - 3:(c + 1) * 512],
                                     waits=w_halo + [xmc_tok[j]])
                        t_z3 = None
                    t0 = p.op("vector", lambda e, i=i, j=j: e.tensor_scalar(
                        cacc[j][:, :], xmc[j][:, 0:512],
                        convw[:, 4 * i:4 * i + 1],
                        None, ALU.mult),
                        waits=[t_ld, t_z3, tok_cst, cacc_tok[j]])
                    for tap in range(1, 4):
                        t0 = p.op("vector", lambda e, i=i, j=j, tap=tap:
                                  e.scalar_tensor_tensor(
                                      cacc[j][:, :], xmc[j][:, tap:512 + tap],
                                      convw[:, 4 * i + tap:4 * i + tap + 1],
                                      cacc[j][:, :], ALU.mult, ALU.add),
                                  waits=[t0])
                    xmc_tok[j] = t0
                    t_u = p.op("scalar", lambda e, i=i, j=j: e.activation(
                        ucv[i][:, :], cacc[j][:, :], AF.Silu,
                        bias=convb[:, i:i + 1]),
                        waits=[t0, ucv_cons, u_st_tok.get(i)])
                    cacc_tok[j] = t_u
                    ev_tok[("u", i)] = t_u
                    u_st_tok[i] = p.dma(
                        "sync", "e_ust", self.d["u_scr"][i * 128:(i + 1) * 128,
                                                         c * 512:(c + 1) * 512],
                        ucv[i][:, :], waits=[t_u])
                def mmx(e):
                    r = None
                    for kb in range(KB):
                        r = e.matmul(self.pb[2][0:65, :], wxpsb[kb][:, :],
                                     ucv[kb][:, :],
                                     start=(kb == 0), stop=(kb == KB - 1))
                    return r

                t_pp = p.op("tensor", mmx, waits=[
                    tok_cst, prev_pp] + [ev_tok[("u", i)] for i in range(KB)])
                ucv_cons = t_pp
                t_pr = p.op("scalar", lambda e: e.copy(
                    prow[0:1, :], self.pb[2][64:65, :]), waits=[t_pp, t_db_prev])
                t_ab = p.op("scalar", lambda e: e.activation(
                    drow[0:1, 0:512], prow[0:1, :], AF.Abs), waits=[t_pr])
                t_ex = p.op("scalar", lambda e: e.activation(
                    drow[0:1, 512:1024], drow[0:1, 0:512], AF.Exp, scale=self.cp(3, 1)),
                    waits=[t_ab] + self.tok_consts)
                t_re = p.op("vector", lambda e: e.tensor_scalar(
                    drow[0:1, 1024:1536], prow[0:1, :], 0.0, None, ALU.max),
                    waits=[t_pr])
                t_ln = p.op("scalar", lambda e: e.activation(
                    drow[0:1, 512:1024], drow[0:1, 512:1024], AF.Ln, bias=self.cp(4, 1)),
                    waits=[t_ex])
                t_dl = p.op("vector", lambda e: e.tensor_tensor(
                    drow[0:1, 1536:2048], drow[0:1, 1024:1536], drow[0:1, 512:1024], ALU.add),
                    waits=[t_ln, t_re, dst_tok])
                dst_tok = p.dma("sync", "e_dst",
                                self.d["delta_scr"][0:1,
                                                    c * 512:(c + 1) * 512],
                                drow[0:1, 1536:2048], waits=[t_dl])
                t_b16 = p.op("tensor", lambda e: e.matmul(
                    self.pb[3][0:16, :], self.sb["onesrow"][0:1, 0:16],
                    drow[0:1, 1536:2048], start=True, stop=True),
                    waits=[t_dl, t_db_prev] + self.tok_consts)
                t_bc16 = p.op("scalar", lambda e: e.copy(
                    dbc16[:, :], self.pb[3][0:16, :]), waits=[t_b16])
                t_bcp = p.op("scalar", lambda e: e.copy(
                    bsb[:, :], self.pb[2][0:16, :]), waits=[t_pp])
                t_db = p.op("vector", lambda e: e.tensor_tensor(
                    dbb[:, :], bsb[:, :], dbc16[:, :], ALU.mult),
                    waits=[t_bc16, t_bcp, dbst_tok])
                t_db_prev = t_db
                dbst_tok = p.dma("sync", "e_dbst",
                                 self.d["db_scr"][0:16,
                                                  c * 512:(c + 1) * 512],
                                 dbb[:, :], waits=[t_db])
                t_csb = p.op("scalar", lambda e: e.copy(
                    csb[:, :], self.pb[2][32:48, :]),
                    waits=[t_pp, cst_tok])
                t_csb_prev = t_csb
                cst_tok = p.dma("sync", "e_cstr",
                                self.d["c_scr"][0:16,
                                                c * 512:(c + 1) * 512],
                                csb[:, :], waits=[t_csb])
                prev_pp = t_csb

    def phase_F(self):
        T, p = self.T, self.p
        NCH = T // 512
        NSB = 8
        with self.scope() as sc:
            Asb = sc.sbuf("s_A", [128, 128], F32)
            logA = sc.sbuf("s_logA", [128, 128], F32)
            dparam = sc.sbuf("s_dp", [128, KB], F32)
            repm = self.sb["repmat"]
            ucs = [sc.sbuf(f"s_u{i}", [128, 512], F32) for i in range(KB)]
            zcs = [sc.sbuf(f"s_z{i}", [128, 512], BF16) for i in range(KB)]
            ya = [sc.sbuf(f"s_y{i}", [128, 512], F32) for i in range(KB)]
            dbt = sc.sbuf("s_dbt", [16, 512], F32)
            ct = sc.sbuf("s_ct", [16, 512], F32)
            drow = sc.sbuf("s_drow", [1, 512], F32)
            dl128 = sc.sbuf("s_dl128", [128, 512], F32)
            at = [sc.sbuf(f"s_at{j}", [128, 512], F32) for j in range(NSB)]
            but = [sc.sbuf(f"s_bu{j}", [128, 512], F32) for j in range(NSB)]
            ht = [sc.sbuf(f"s_ht{j}", [128, 512], F32) for j in range(NSB)]
            cht = [sc.sbuf(f"s_cht{j}", [128, 512], F32) for j in range(NSB)]
            st = [sc.sbuf(f"s_st{i}", [128, 16], F32) for i in range(KB)]
            zsl = [sc.sbuf(f"s_zsl{j}", [128, 512], F32) for j in range(2)]
            y2b = [sc.sbuf(f"s_y2{j}", [128, 512], BF16) for j in range(2)]
            xnf = [sc.sbuf(f"f_xnf{i}", [128, 512], BF16) for i in range(KB)]
            wfb = [sc.sbuf(f"f_wfb{j}", [128, KB * 128], BF16) for j in range(2)]
            hh = [sc.sbuf(f"f_hh{m}", [128, 512], BF16) for m in range(HB)]
            hsq = [sc.sbuf(f"f_hsq{j}", [128, 512], BF16) for j in range(2)]
            srow = sc.sbuf("f_srow", [1, 512], F32)
            sqrow = sc.sbuf("f_sqrow", [1, 512], F32)
            stat = sc.sbuf("f_stat", [1, 512], F32)
            mrow = sc.sbuf("f_mrow", [1, 512], F32)
            rrow = sc.sbuf("f_rrow", [1, 512], F32)
            mrr = sc.sbuf("f_mrr", [1, 512], F32)
            rbc = sc.sbuf("f_rbc", [128, 512], F32)
            mbc = sc.sbuf("f_mbc", [128, 512], F32)
            hta = [sc.sbuf(f"f_hta{j}", [128, 512], F32) for j in range(2)]
            hnb = [sc.sbuf(f"f_hnb{j}", [128, 512], BF16) for j in range(2)]
            lnw = sc.sbuf("f_lnw", [128, HB], F32)
            lnb = sc.sbuf("f_lnb", [128, HB], F32)

            t_ln1 = p.dma("sync", "f_lnl", lnw[:, :], self.d["lnw"])
            t_ln2 = p.dma("sync", "f_lnl", lnb[:, :], self.d["lnb"])
            tok_ln = self.p.dtok("f_lnl")
            t_la = p.dma("sync", "s_cst", logA[:, :], self.d["logA"])
            t_dp = p.dma("sync", "s_cst", dparam[:, :], self.d["dparam"])
            tok_cst = self.p.dtok("s_cst")
            t_A1 = p.op("scalar", lambda e: e.activation(
                Asb[:, :], logA[:, :], AF.Exp), waits=[tok_cst])
            t_A = p.op("vector", lambda e: e.tensor_scalar(
                Asb[:, :], Asb[:, :], -1.0, None, ALU.mult), waits=[t_A1])

            u_cons = None
            z_cons = None
            misc_cons = None
            sv_guard = [None] * NSB
            dbbc_read = [None, None]
            cbc_read = [None, None]
            bc_guard = [None, None]
            dl_guard = None
            abuf_tok = [None] * NSB
            bbuf_tok = [None] * NSB
            hbuf_tok = [None] * NSB
            chbuf_tok = [None] * NSB
            y2_store = [None, None]
            zsl_tok = [None, None]
            state_tok = {}
            ych_tok = {}
            xnf_cons = None
            wf_tok = [None, None]
            ev_hh = {}
            sq_tok = [None, None]
            stat_guard = None
            hn_store = [None, None]
            hn_tok = [None, None]
            prev_hh_guard = []
            bct = {4: None, 5: None, 6: None, 7: None}
            for c in range(NCH):
                tu = None
                tz = None
                for i in range(KB):
                    tu = p.dma("sync", "s_uld", ucs[i][:, :],
                               self.d["u_scr"][i * 128:(i + 1) * 128,
                                               c * 512:(c + 1) * 512],
                               waits=[u_cons])
                    tz = p.dma("sync", "s_zld", zcs[i][:, :],
                               self.d["z_scr"][i * 128:(i + 1) * 128,
                                               c * 512:(c + 1) * 512],
                               waits=[z_cons])
                tm = p.dma("sync", "s_mld", dbt[:, :],
                           self.d["db_scr"][0:16, c * 512:(c + 1) * 512],
                           waits=[misc_cons])
                tm2 = p.dma("sync", "s_mld", ct[:, :],
                            self.d["c_scr"][0:16, c * 512:(c + 1) * 512],
                            waits=[misc_cons])
                tm3 = p.dma("sync", "s_mld", drow[0:1, :],
                            self.d["delta_scr"][0:1, c * 512:(c + 1) * 512],
                            waits=[misc_cons])
                t_dbc = p.op("tensor", lambda e: e.matmul(
                    self.pb[6][:, :], self.sb["onesrow"][0:1, :], drow[0:1, :],
                    start=True, stop=True),
                    waits=[tm3, dl_guard, bct[6]] + self.tok_consts)
                t_dl = p.op("scalar", lambda e: e.copy(
                    dl128[:, :], self.pb[6][:, :]), waits=[t_dbc])
                bct[6] = t_dl
                txn = None
                for i in range(KB):
                    txn = p.dma("sync", "f_xnfl", xnf[i][:, :],
                                self.d["xn1_tm"][c * 512:(c + 1) * 512,
                                                 i * 128:(i + 1) * 128],
                                waits=[xnf_cons], transpose=True)
                t_sum = None
                t_ssq = None
                for n in range(16):
                    jn = n % 2
                    bA = 4 + (2 * n) % 4
                    bB = 4 + (2 * n + 1) % 4
                    t_n1 = p.op("tensor", lambda e, n=n, bA=bA: e.matmul(
                        self.pb[bA][:, :], repm[:, n * 128:(n + 1) * 128],
                        dbt[:, :], start=True, stop=True),
                        waits=[tm, bct[bA]] + self.tok_consts)
                    t_n3 = p.op("tensor", lambda e, n=n, bB=bB: e.matmul(
                        self.pb[bB][:, :], repm[:, n * 128:(n + 1) * 128],
                        ct[:, :], start=True, stop=True),
                        waits=[tm2, bct[bB]])
                    for i in range(KB):
                        j = (n * KB + i) % NSB
                        col = i * 16 + n
                        t_a = p.op("scalar", lambda e, j=j, col=col: e.activation(
                            at[j][:, :], dl128[:, :], AF.Exp,
                            scale=Asb[:, col:col + 1]),
                            waits=[t_dl, t_A, abuf_tok[j]])
                        t_bu = p.op("vector", lambda e, j=j, i=i, bA=bA:
                                    e.tensor_tensor(but[j][:, :], ucs[i][:, :],
                                                    self.pb[bA][:, :], ALU.mult),
                                    waits=[tu, t_n1, bbuf_tok[j]])
                        bct[bA] = t_bu
                        init = (0.0 if c == 0 else st[i][:, n:n + 1])
                        t_h = p.op("vector", lambda e, j=j, init=init:
                                   e.tensor_tensor_scan(
                                       ht[j][:, :], at[j][:, :], but[j][:, :],
                                       init, ALU.mult, ALU.add),
                                   waits=[t_a, t_bu, hbuf_tok[j],
                                          sv_guard[j],
                                          state_tok.get((c - 1, i, n))])
                        abuf_tok[j] = t_h
                        bbuf_tok[j] = t_h
                        t_sv = p.op("vector", lambda e, j=j, i=i, n=n:
                                    e.tensor_copy(st[i][:, n:n + 1],
                                                  ht[j][:, 511:512]),
                                    waits=[t_h])
                        state_tok[(c, i, n)] = t_sv
                        sv_guard[j] = t_sv
                        t_ch = p.op("vector", lambda e, j=j, bB=bB:
                                    e.tensor_tensor(cht[j][:, :], ht[j][:, :],
                                                    self.pb[bB][:, :], ALU.mult),
                                    waits=[t_h, t_n3, chbuf_tok[j]])
                        bct[bB] = t_ch
                        hbuf_tok[j] = t_ch
                        if n == 0:
                            t_y = p.op("vector", lambda e, j=j, i=i:
                                       e.tensor_copy(ya[i][:, :], cht[j][:, :]),
                                       waits=[t_ch, ych_tok.get(("fin", i))])
                        else:
                            t_y = p.op("vector", lambda e, j=j, i=i:
                                       e.tensor_tensor(ya[i][:, :], ya[i][:, :],
                                                       cht[j][:, :], ALU.add),
                                       waits=[t_ch, ych_tok.get((n - 1, i))])
                        ych_tok[(n, i)] = t_y
                        chbuf_tok[j] = t_y
                    for m in (2 * n, 2 * n + 1):
                        j = m % 2
                        tl = p.dma("sync", f"f_wfl{j}", wfb[j][:, :],
                                   self.d["wfc16m"][m], waits=[wf_tok[j]])
                        bank = j

                        def mmf(e, j=j, bank=bank):
                            r = None
                            for kb in range(KB):
                                r = e.matmul(self.pb[bank][:, :],
                                             wfb[j][:, kb * 128:(kb + 1) * 128],
                                             xnf[kb][:, :],
                                             start=(kb == 0),
                                             stop=(kb == KB - 1))
                            return r

                        tmm = p.op("tensor", mmf,
                                   waits=[tl, txn, ev_hh.get((m - 2) % HB)])
                        wf_tok[j] = tmm
                        ev_hh[m] = p.op("scalar", lambda e, m=m, bank=bank:
                                        e.activation(hh[m][:, :],
                                                     self.pb[bank][:, :],
                                                     AF.Gelu),
                                        waits=[tmm] + prev_hh_guard)
                        t_q = p.op("scalar", lambda e, m=m, j=j: e.activation(
                            hsq[j][:, :], hh[m][:, :], AF.Square),
                            waits=[ev_hh[m], sq_tok[j]])
                        t_sum = p.op("tensor", lambda e, m=m: e.matmul(
                            self.pb[2][0:1, :], self.sb["onescol16"][:, 0:1],
                            hh[m][:, :], start=(m == 0), stop=(m == HB - 1)),
                            waits=[ev_hh[m], stat_guard] + self.tok_consts)
                        t_ssq = p.op("tensor", lambda e, m=m, j=j: e.matmul(
                            self.pb[3][0:1, :], self.sb["onescol16"][:, 0:1],
                            hsq[j][:, :], start=(m == 0), stop=(m == HB - 1)),
                            waits=[t_q])
                        sq_tok[j] = t_ssq
                for i in range(KB):
                    j = i % 2
                    t_ud = p.op("vector", lambda e, i=i: e.scalar_tensor_tensor(
                        ya[i][:, :], ucs[i][:, :], dparam[:, i:i + 1],
                        ya[i][:, :], ALU.mult, ALU.add),
                        waits=[ych_tok[(15, i)], tok_cst])
                    t_sz = p.op("scalar", lambda e, i=i, j=j: e.activation(
                        zsl[j][:, :], zcs[i][:, :], AF.Silu),
                        waits=[tz, zsl_tok[j]])
                    t_y2 = p.op("vector", lambda e, i=i, j=j: e.tensor_tensor(
                        y2b[j][:, :], ya[i][:, :], zsl[j][:, :], ALU.mult),
                        waits=[t_ud, t_sz, y2_store[j]])
                    zsl_tok[j] = t_y2
                    ych_tok[("fin", i)] = t_y2
                    y2_store[j] = p.dma(
                        "sync", f"s_y2st{j}",
                        self.d["y2_scr"][i * 128:(i + 1) * 128,
                                         c * 512:(c + 1) * 512],
                        y2b[j][:, :], waits=[t_y2])
                u_cons = p.tok("vector")
                z_cons = p.tok("scalar")
                misc_cons = p.tok("tensor")
                dl_guard = p.tok("scalar")
                xnf_cons = p.tok("tensor")
                xnf_cons = p.tok("tensor")
                t_sr = p.op("scalar", lambda e: e.copy(
                    srow[0:1, :], self.pb[2][0:1, :]),
                    waits=[t_sum] + prev_hh_guard)
                t_qr = p.op("scalar", lambda e: e.copy(
                    sqrow[0:1, :], self.pb[3][0:1, :]), waits=[t_ssq])
                t_m = p.op("vector", lambda e: e.tensor_scalar(
                    mrow[0:1, :], srow[0:1, :], 1.0 / HID, None, ALU.mult),
                    waits=[t_sr])
                t_m2 = p.op("vector", lambda e: e.tensor_tensor(
                    stat[0:1, :], mrow[0:1, :], mrow[0:1, :], ALU.mult),
                    waits=[t_m])
                t_v = p.op("vector", lambda e: e.scalar_tensor_tensor(
                    stat[0:1, :], sqrow[0:1, :], 1.0 / HID, stat[0:1, :],
                    ALU.mult, ALU.subtract), waits=[t_qr, t_m2])
                t_l = p.op("scalar", lambda e: e.activation(
                    stat[0:1, :], stat[0:1, :], AF.Ln, bias=self.cp(0, 1)),
                    waits=[t_v] + self.tok_consts)
                t_r = p.op("scalar", lambda e: e.activation(
                    rrow[0:1, :], stat[0:1, :], AF.Exp, scale=self.cp(1, 1)),
                    waits=[t_l])
                t_mr = p.op("vector", lambda e: e.tensor_tensor(
                    mrr[0:1, :], mrow[0:1, :], rrow[0:1, :], ALU.mult),
                    waits=[t_r])
                t_b1 = p.op("tensor", lambda e: e.matmul(
                    self.pb[2][:, :], self.sb["onesrow"][0:1, :], rrow[0:1, :],
                    start=True, stop=True), waits=[t_r, t_sr, t_qr])
                t_b2 = p.op("tensor", lambda e: e.matmul(
                    self.pb[3][:, :], self.sb["onesrow"][0:1, :], mrr[0:1, :],
                    start=True, stop=True), waits=[t_mr])
                t_rb = p.op("scalar", lambda e: e.copy(
                    rbc[:, :], self.pb[2][:, :]),
                    waits=[t_b1] + prev_hh_guard)
                t_mb = p.op("scalar", lambda e: e.copy(
                    mbc[:, :], self.pb[3][:, :]), waits=[t_b2])
                stat_guard = t_mb
                for m in range(HB):
                    j = m % 2
                    t1 = p.op("vector", lambda e, m=m, j=j: e.tensor_tensor(
                        hta[j][:, :], hh[m][:, :], rbc[:, :], ALU.mult),
                        waits=[t_rb, hn_tok[j]])
                    t2 = p.op("vector", lambda e, j=j: e.tensor_tensor(
                        hta[j][:, :], hta[j][:, :], mbc[:, :], ALU.subtract),
                        waits=[t1, t_mb])
                    t3 = p.op("vector", lambda e, m=m, j=j: e.tensor_scalar(
                        hnb[j][:, :], hta[j][:, :], lnw[:, m:m + 1],
                        lnb[:, m:m + 1], ALU.mult, ALU.add),
                        waits=[t2, tok_ln, hn_store[j]])
                    hn_tok[j] = t3
                    hn_store[j] = p.dma(
                        "sync", f"f_hnst{j}",
                        self.d["hn_scr"][m * 128:(m + 1) * 128,
                                         c * 512:(c + 1) * 512],
                        hnb[j][:, :], waits=[t3])
                prev_hh_guard = [p.tok("vector"), p.tok("tensor"),
                                 p.tok("scalar")]

    def phase_G(self):
        T, p = self.T, self.p
        NCH = T // 512
        with self.scope() as sc:
            y2f = [sc.sbuf(f"g_y2{i}", [128, 512], BF16) for i in range(KB)]
            hnf = [sc.sbuf(f"g_hn{i}", [128, 512], BF16) for i in range(HB)]
            wosb = [sc.sbuf(f"g_wo{i}", [128, D], BF16) for i in range(KB)]
            wpsb = [sc.sbuf(f"g_wp{i}", [128, D], BF16) for i in range(HB)]
            s1sl = sc.sbuf("g_s1", [128, D], F32)
            g1sl = sc.sbuf("g_g1", [128, D], F32)
            sgt = sc.sbuf("g_sg", [128, D], F32)
            osl = [sc.sbuf(f"g_o{j}", [128, D], F32) for j in range(2)]
            xsl = [sc.sbuf(f"g_x{j}", [128, D], F32) for j in range(2)]

            t_wo = None
            for i in range(KB):
                t_wo = p.dma("sync", "g_wol", wosb[i][:, :],
                             self.d["wout16"][i * 128:(i + 1) * 128, :],
                             waits=[self.wwait("wout")])
            t_wp = None
            for i in range(HB):
                t_wp = p.dma("sync", "g_wpl", wpsb[i][:, :],
                             self.d["wproj16"][i * 128:(i + 1) * 128, :])

            y2_cons = None
            hn_cons = None
            ev_guard2 = [None, None]
            sg_guard = None
            t_o_prev = None
            out_store = [None, None]
            for c in range(NCH):
                ty = None
                for i in range(KB):
                    ty = p.dma("sync", "g_y2l", y2f[i][:, :],
                               self.d["y2_scr"][i * 128:(i + 1) * 128,
                                                c * 512:(c + 1) * 512],
                               waits=[y2_cons])
                th = None
                for i in range(HB):
                    th = p.dma("sync", "g_hnl", hnf[i][:, :],
                               self.d["hn_scr"][i * 128:(i + 1) * 128,
                                                c * 512:(c + 1) * 512],
                               waits=[hn_cons])
                for ts in range(4):
                    jb = ts % 2

                    def mms(e, ts=ts, jb=jb):
                        r = None
                        for kb in range(KB):
                            e.matmul(self.pb[jb][:, :],
                                     y2f[kb][:, ts * 128:(ts + 1) * 128],
                                     wosb[kb][:, 0:512],
                                     start=(kb == 0), stop=(kb == KB - 1))
                            r = e.matmul(self.pb[4 + jb][:, :],
                                         y2f[kb][:, ts * 128:(ts + 1) * 128],
                                         wosb[kb][:, 512:1024],
                                         start=(kb == 0), stop=(kb == KB - 1))
                        return r

                    def mmg(e, ts=ts, jb=jb):
                        r = None
                        for kb in range(HB):
                            e.matmul(self.pb[2 + jb][:, :],
                                     hnf[kb][:, ts * 128:(ts + 1) * 128],
                                     wpsb[kb][:, 0:512],
                                     start=(kb == 0), stop=(kb == HB - 1))
                            r = e.matmul(self.pb[6 + jb][:, :],
                                         hnf[kb][:, ts * 128:(ts + 1) * 128],
                                         wpsb[kb][:, 512:1024],
                                         start=(kb == 0), stop=(kb == HB - 1))
                        return r

                    tms = p.op("tensor", mms, waits=[ty, t_wo, ev_guard2[jb]])
                    tmg = p.op("tensor", mmg, waits=[th, t_wp, ev_guard2[jb]])
                    t_s1 = p.op("scalar", lambda e, jb=jb: (
                        e.copy(s1sl[:, 0:512], self.pb[jb][:, :]),
                        e.copy(s1sl[:, 512:1024], self.pb[4 + jb][:, :]))[-1],
                        waits=[tms, sg_guard])
                    t_g1 = p.op("scalar", lambda e, jb=jb: (
                        e.activation(g1sl[:, 0:512], self.pb[2 + jb][:, :],
                                     AF.Silu),
                        e.activation(g1sl[:, 512:1024], self.pb[6 + jb][:, :],
                                     AF.Silu))[-1], waits=[tmg, sg_guard])
                    ev_guard2[jb] = t_g1
                    tx = p.dma("sync", f"g_xl{ts % 2}", xsl[ts % 2][:, :],
                               self.d["x1_scr"][c * 512 + ts * 128:
                                                c * 512 + (ts + 1) * 128, :],
                               waits=[sg_guard])
                    t_sg = p.op("vector", lambda e: e.tensor_tensor(
                        sgt[:, :], s1sl[:, :], g1sl[:, :], ALU.mult),
                        waits=[t_s1, t_g1, t_o_prev])
                    sg_guard = t_sg
                    t_o = p.op("vector", lambda e, ts=ts: e.tensor_tensor(
                        osl[ts % 2][:, :], sgt[:, :], xsl[ts % 2][:, :],
                        ALU.add), waits=[t_sg, tx, out_store[ts % 2]])
                    t_o_prev = t_o
                    out_store[ts % 2] = p.dma(
                        "sync", f"g_ost{ts % 2}",
                        self.d["out"][c * 512 + ts * 128:
                                      c * 512 + (ts + 1) * 128, :],
                        osl[ts % 2][:, :], waits=[t_o])
                y2_cons = p.tok("tensor")
                hn_cons = p.tok("tensor")



def build(T, phases):
    ctx = ExitStack()
    k = Kern(T, phases, ctx)
    with ctx:
        for ph in phases:
            getattr(k, "phase_" + ph)()
        k.finalize()
    return k


def run_groups(inputs, T, groups, n_cores=8):
    x_full = np.asarray(inputs["L0_attn_x"])
    B = x_full.shape[0]
    assert x_full.shape[1] >= T
    w = prep_weights(inputs)
    consts = host_consts()
    carry = [{} for _ in range(n_cores)]
    outs = None
    for phases in groups:
        k = build(T, phases)
        in_maps = []
        for ci in range(n_cores):
            m = {}
            for nm in k.in_names:
                if nm == "x":
                    m[nm] = np.ascontiguousarray(x_full[ci % B, :T, :])
                elif nm in consts:
                    m[nm] = consts[nm]
                elif nm in w:
                    m[nm] = w[nm]
                else:
                    m[nm] = carry[ci][nm]
            in_maps.append(m)
        res = run_bass_kernel_spmd(k.nc, in_maps, core_ids=list(range(n_cores)))
        for ci in range(n_cores):
            for nm in k.out_names:
                carry[ci][nm] = res.results[ci][nm]
        if "G" in phases:
            outs = [carry[ci]["out"] for ci in range(n_cores)]
    return outs


def kernel(**inputs):
    T = 4096
    outs = run_groups(inputs, T, [PHASES_ALL], n_cores=8)
    return np.stack([np.asarray(o, np.float32) for o in outs], axis=0)
